# version 02 (final)
# baseline (speedup 1.0000x reference)
import numpy as np
import ml_dtypes
from contextlib import ExitStack

import concourse.bass as bass
import concourse.mybir as mybir
from concourse.bass_utils import run_bass_kernel_spmd

F32 = mybir.dt.float32
BF16 = mybir.dt.bfloat16
I32 = mybir.dt.int32
AF = mybir.ActivationFunctionType
ALU = mybir.AluOpType
AX = mybir.AxisListType

B, S, D = 4, 8192, 1024
SH = S // 2
NT = 64
NTO = 32
NG = 16
NGO = 8
NB = 16
PHASES = ("PRE", "P1A", "P1B", "P15", "P2", "P3A", "P3B")
P1A_LEVEL = 9
ENGINES = ("sync", "gpsimd", "tensor", "vector", "scalar")


class Step:
    __slots__ = ("eng", "emit", "waits", "ev", "inc")

    def __init__(self, eng, emit, waits, ev, inc):
        self.eng, self.emit, self.waits, self.ev, self.inc = \
            eng, emit, waits, ev, inc


class Sched:

    def __init__(self, sems, makesem):
        self.sems = sems
        self.makesem = makesem
        self.steps = {e: [] for e in ENGINES}
        self.events = {}
        self.slot_eng = {}
        self._auto = 0

    def _slot(self, name, eng):
        pool = self.sems.setdefault("_slotmap", {})
        key = (eng, name)
        if key not in pool:
            used = sum(1 for k in pool if k[0] == eng)
            phys = f"dq_{eng}_{used % 5}"
            pool[key] = phys
            if phys not in self.sems:
                self.sems[phys] = self.makesem(phys)
            self.slot_eng[phys] = eng
        return pool[key]

    def step(self, eng, emit, waits=(), ev=None, dma=None, coll=None):
        if ev is None:
            self._auto += 1
            ev = f"_e{self._auto}"
        assert ev not in self.events, ev
        if dma:
            ch = self._slot("dma_" + dma, eng)
            inc = 16
        elif coll:
            ch = self._slot("coll_" + coll, eng)
            inc = 1
        else:
            ch = eng
            inc = 1
        st = Step(ch, emit, tuple(w for w in waits if w is not None), ev, inc)
        self.steps[eng].append(st)
        self.events[ev] = (ch, None)
        return ev

    def emit(self, blk):
        cnt = {}
        for e in ENGINES:
            for st in self.steps[e]:
                cnt.setdefault(st.eng, self.sems[st.eng].n)
                cnt[st.eng] += st.inc
                self.events[st.ev] = (st.eng, cnt[st.eng])
        for ename in ENGINES:
            if not self.steps[ename]:
                continue

            def body(eng, _steps=self.steps[ename], _ename=ename):
                for st in _steps:
                    if st.eng not in ENGINES:
                        eng.wait_ge(self.sems[st.eng].h, self.sems[st.eng].n)
                    for w in st.waits:
                        wch, wval = self.events[w]
                        eng.wait_ge(self.sems[wch].h, wval)
                    ins = st.emit(eng)
                    assert ins is not None, f"{st.ev} on {_ename} returned None"
                    sem = self.sems[st.eng]
                    ins.then_inc(sem.h, st.inc)
                    sem.n += st.inc
                eng.wait_ge(self.sems[_ename].h, self.sems[_ename].n)
                for sl, se in self.slot_eng.items():
                    if se == _ename:
                        eng.wait_ge(self.sems[sl].h, self.sems[sl].n)
            getattr(blk, ename)(body)


class Sem:
    def __init__(self, handle):
        self.h = handle
        self.n = 0


def strided(ap, extra_off, dims):
    return bass.AP(ap.tensor, ap.offset + extra_off, [ap.ap[0]] + list(dims))


def build(upto="P3B", debug=False):
    active = set(PHASES[:PHASES.index(upto) + 1])
    nc = bass.Bass("TRN2", target_bir_lowering=False,
                   detect_race_conditions=False)

    def inp(name, shape, dt=F32):
        return nc.dram_tensor(name, list(shape), dt, kind="ExternalInput").ap()

    i_xf = inp("xf", (S, D))
    i_xo = inp("xo", (SH, D))
    i_wq = inp("wq", (D, 256))
    i_wk = inp("wk", (D, 256))
    i_wv = inp("wv", (D, 512))
    i_wo = inp("wo", (D, 512))
    i_wif = inp("wif", (D, 8))
    i_bif = inp("bif", (1, 8))
    i_nodew = inp("nodew", (D, 48))
    i_nodeb = inp("nodeb", (1, 48))
    i_w1 = {f: inp(f"w1{f}", (D, D)) for f in "qkv"}
    i_b1 = {f: inp(f"b1{f}", (1, D)) for f in "qkv"}
    i_w2 = {f: inp(f"w2{f}", (D, D)) for f in "qkv"}
    i_b2 = {f: inp(f"b2{f}", (16, D)) for f in "qkv"}
    i_wdown = inp("wdown", (D, D))
    i_onorm = inp("onorm", (128, 4))
    i_idbf = inp("idbf", (128, 128), BF16)
    i_idf32 = inp("idf32", (128, 128))
    i_onesbf = inp("onesbf", (128, 128), BF16)
    i_masks = inp("masks", (128, 4 * 512), BF16)
    i_iota4 = inp("iota4", (128, 4))
    i_iota16 = inp("iota16", (128, 16))
    i_offg = inp("offg", (1, 1), I32)
    i_offr = inp("offr", (1, 1), I32)

    o_y = nc.dram_tensor("y", [SH, D], F32, kind="ExternalOutput").ap()
    dbg = {}

    def dbgout(name, shape, dt=F32):
        if debug and name not in dbg:
            dbg[name] = nc.dram_tensor("dbg_" + name, list(shape), dt,
                                       kind="ExternalOutput").ap()
        return dbg.get(name)

    def scr(name, shape, dt, shared=False):
        kw = dict(kind="Internal")
        if shared:
            kw["addr_space"] = "Shared"
        return nc.dram_tensor(name, list(shape), dt, **kw).ap()

    s_sigoT = scr("s_sigoT", (512, S), BF16)
    s_mixqT = scr("s_mixqT", (256, S), BF16)
    s_mixkT = scr("s_mixkT", (256, S), BF16)
    s_mixv = scr("s_mixv", (4, S, 128), BF16)
    s_ffnqT = scr("s_ffnqT", (D, SH), BF16)
    s_ffnkT = scr("s_ffnkT", (D, S), BF16)
    s_ffnv = scr("s_ffnv", (S, D), BF16)
    s_gatedT = scr("s_gatedT", (512, S), BF16)
    g_gath = [scr(f"g_gath{i}", (8 * 128, S), BF16, shared=True)
              for i in range(4)]
    s_pA = scr("s_pA", (NTO, 128, D + 4), F32)
    s_gg = scr("s_gg", (4, 256, SH), BF16)

    CTX = ExitStack()
    with CTX:
        sbuf = lambda n, sh, dt: CTX.enter_context(
            nc.sbuf_tensor(n, list(sh), dt))

        id_bf = sbuf("id_bf", (128, 128), BF16)
        id_f32 = sbuf("id_f32", (128, 128), F32)
        ones_bf = sbuf("ones_bf", (128, 128), BF16)
        onorm = sbuf("sb_onorm", (128, 4), F32)
        iota4 = sbuf("sb_iota4", (128, 4), F32)
        iota16 = sbuf("sb_iota16", (128, 16), F32)
        gcol = sbuf("gcol", (128, NT * 12), F32)
        crep = sbuf("crep", (128, 64), F32)
        ksum_sb = sbuf("ksum_sb", (2, D), F32)
        kbar_bf = sbuf("kbar_bf", (128, 8), BF16)
        soff = sbuf("sb_soff", (1, 1), I32)
        soffr = sbuf("sb_soffr", (1, 1), I32)
        constc = sbuf("constc", (128, 4), F32)
        ESC = ExitStack()
        esbuf = lambda n, sh, dt: ESC.enter_context(
            nc.sbuf_tensor(n, list(sh), dt, side="right"))
        w_q = esbuf("w_q", (128, 8 * 256), BF16)
        w_k = esbuf("w_k", (128, 8 * 256), BF16)
        w_v = esbuf("w_v", (128, 8 * 512), BF16)
        w_o = esbuf("w_o", (128, 8 * 512), BF16)
        w_small = esbuf("w_small", (128, 8 * 64), BF16)
        b_small = esbuf("b_small", (1, 64), BF16)
        masks = esbuf("sb_masks", (128, 4 * 512), BF16)
        gpre = esbuf("gpre", (36, S), F32)

        sems = {e: Sem(CTX.enter_context(nc.semaphore("sem_" + e)))
                for e in ENGINES}

        def makesem(name):
            return Sem(CTX.enter_context(nc.semaphore(name)))
        regs = {}

        def D_(sch, dst_ap, src_ap, eng="sync", waits=(), ev=None,
               slot="misc"):
            return sch.step(eng, lambda e, d=dst_ap, so=src_ap:
                            e.dma_start(out=d, in_=so), waits=waits, ev=ev,
                            dma=slot)

        sch = Sched(sems, makesem)
        D_(sch, id_bf[:], i_idbf, slot="preS")
        D_(sch, id_f32[:], i_idf32, slot="preS")
        D_(sch, ones_bf[:], i_onesbf, slot="preS")
        D_(sch, onorm[:], i_onorm, slot="preS")
        D_(sch, masks[:], i_masks, slot="preS")
        D_(sch, iota4[:], i_iota4, slot="preS")
        D_(sch, iota16[:], i_iota16, slot="preS")
        evoff = D_(sch, soff[:], i_offg, slot="preS")
        evoffr = D_(sch, soffr[:], i_offr, slot="preS")
        D_(sch, w_q[:].rearrange("p (c f) -> p c f", f=256), i_wq.rearrange("(c p) f -> p c f", p=128), "gpsimd", slot="preG")
        D_(sch, w_k[:].rearrange("p (c f) -> p c f", f=256), i_wk.rearrange("(c p) f -> p c f", p=128), "gpsimd", slot="preG")
        D_(sch, w_v[:].rearrange("p (c f) -> p c f", f=512), i_wv.rearrange("(c p) f -> p c f", p=128), "gpsimd", slot="preG")
        D_(sch, w_o[:].rearrange("p (c f) -> p c f", f=512), i_wo.rearrange("(c p) f -> p c f", p=128), "gpsimd", slot="preG")
        wsm = w_small[:].rearrange("p (c f) -> p c f", f=64)
        D_(sch, wsm[:, :, 0:8], i_wif.rearrange("(c p) f -> p c f", p=128),
           "gpsimd", slot="preG")
        D_(sch, wsm[:, :, 8:56], i_nodew.rearrange("(c p) f -> p c f", p=128),
           "gpsimd", slot="preG")
        D_(sch, b_small[0:1, 0:8], i_bif, "gpsimd", slot="preG")
        D_(sch, b_small[0:1, 8:56], i_nodeb, "gpsimd", slot="preG")
        sch.step("gpsimd", lambda e: e.memset(wsm[:, :, 56:64], 0))
        sch.step("gpsimd", lambda e: e.memset(b_small[0:1, 56:64], 0))

        def pre_regs(e):
            r = CTX.enter_context(e.register("r_goff"))
            regs["goff"] = r
            e.reg_load(r, soff[0:1, 0:1])
            regs["gg"] = CTX.enter_context(e.register("r_gg"))
            regs["fb"] = CTX.enter_context(e.register("r_fb"))
            e.reg_load(regs["fb"], soffr[0:1, 0:1])
            return e.nop()
        sch.step("sync", pre_regs, waits=(evoff, evoffr))
        if debug:
            sch.step("vector", lambda e: e.memset(gpre[:], 0.0))
        sch.step("vector", lambda e: e.memset(constc[:, 0:1], 1e-5))
        sch.step("vector", lambda e: e.memset(constc[:, 1:2], 1e-6))
        sch.step("vector", lambda e: e.memset(constc[:, 2:3], 0.5))
        sch.step("vector", lambda e: e.memset(constc[:, 3:4], 1.0))
        with nc.Block("pre") as blk:
            sch.emit(blk)

        def production_phase(own):
            name = "b" if own else "a"
            ngrp = NGO if own else NG
            xsrc = i_xo if own else i_xf
            ffs = ["q"] if own else ["v", "k"]
            P = ExitStack()
            with P:
                sb = lambda n, sh, dt: P.enter_context(
                    nc.sbuf_tensor(name + n, list(sh), dt))
                wf1 = {f: sb(f"w1{f}", (128, 8 * 1024), BF16) for f in ffs}
                wf2 = {f: sb(f"w2{f}", (128, 8 * 1024), BF16) for f in ffs}
                bf1 = {f: sb(f"b1{f}", (1, 1024), BF16) for f in ffs}
                bf2 = {f: sb(f"b2{f}", (16, 1024), BF16) for f in ffs}
                xg = [sb(f"xg{i}", (128, 1024), F32) for i in range(2)]
                xnbf = [sb(f"xnbf{i}", (128, 1024), BF16) for i in range(2)]
                xnT = [sb("xnT", (128, 8 * 512), BF16)][0]
                trash_s = sb("trash_s", (128, 1024), F32)
                cols = sb("cols", (128, 8), F32)
                smallf = sb("smallf", (128, 64), F32)
                sel = sb("sel", (128, 48), F32)
                selbf = sb("selbf", (128, 48), BF16)
                selT = sb("selT", (16, 256), BF16)
                rt = sb("rt", (128, 56), F32)
                hrelu = sb("hrelu", (128, 1024), BF16)
                hidbf = sb("hidbf", (128, 1024), BF16)
                hidT = sb("hidT", (128, 1024), BF16)
                outv = [sb(f"outv{i}", (128, 1024), BF16) for i in range(2)]
                outk = [sb(f"outk{i}", (128, 1024), BF16) for i in range(2)]
                kT = [sb(f"kT{i}", (128, 1024), BF16) for i in range(2)]
                projbf = [sb(f"projbf{i}", (128, 512), BF16) for i in range(4)]
                sigf = [sb(f"sigf{i}", (128, 512), F32) for i in range(2)]
                sigbf = [sb(f"sigbf{i}", (128, 512), BF16) for i in range(4)]

                ps = lambda n, sh, dt: P.enter_context(
                    nc.psum_tensor(name + n, list(sh), dt))
                pTA = ps("pTA", (128, 1024), BF16)
                pSm = ps("pSm", (128, 512), F32)
                pB = [ps(f"pB{i}", (128, 512), F32) for i in range(4)]
                if not own:
                    pKs = [ps(f"pKs{i}", (2, 512), F32) for i in range(2)]

                s2 = Sched(sems, makesem)
                lastc = {}

                def bw(key, *waits):
                    out = [w for w in waits if w]
                    if lastc.get(key):
                        out.append(lastc[key])
                    return tuple(out)

                evw = None
                for f in ffs:
                    D_(s2, wf1[f][:].rearrange("p (c f) -> p c f", f=1024),
                       i_w1[f].rearrange("(c p) f -> p c f", p=128), "gpsimd", slot="ldw")
                    D_(s2, wf2[f][:].rearrange("p (c f) -> p c f", f=1024),
                       i_w2[f].rearrange("(c p) f -> p c f", p=128), "gpsimd", slot="ldw")
                    D_(s2, bf1[f][:], i_b1[f], "gpsimd", slot="ldw")
                    evw = D_(s2, bf2[f][:], i_b2[f], "gpsimd", slot="ldw")

                xfree = {}
                ks_last = None
                for g in range(ngrp):
                    ev_xnT_t = {}
                    for tt in range(4):
                        i = 4 * g + tt
                        par = i % 2
                        exl = D_(s2, xg[par][:],
                                 xsrc[i * 128:(i + 1) * 128, :],
                                 "sync", waits=xfree.get(i - 2, ()),
                                 slot=f"x{par}")
                        xt = xg[par][:]
                        ev_sum = s2.step(
                            "vector",
                            lambda e, xt=xt: e.reduce_sum(
                                cols[:, 0:1], xt, AX.X),
                            waits=bw("cols", exl))
                        ev_sq = s2.step(
                            "scalar",
                            lambda e, xt=xt: e.activation(
                                trash_s[:], xt, AF.Square,
                                accum_out=cols[:, 1:2]),
                            waits=bw("cols", exl))
                        ev_mean = s2.step(
                            "vector",
                            lambda e: e.tensor_scalar(
                                cols[:, 2:3], cols[:, 0:1], 1.0 / 1024, None,
                                ALU.mult),
                            waits=(ev_sum,))
                        ev_q = s2.step(
                            "vector",
                            lambda e: e.tensor_scalar(
                                cols[:, 6:7], cols[:, 1:2], 1.0 / 1024, None,
                                ALU.mult),
                            waits=(ev_sq,))
                        ev_m2 = s2.step(
                            "vector",
                            lambda e: e.tensor_tensor(
                                cols[:, 3:4], cols[:, 2:3], cols[:, 2:3],
                                ALU.mult),
                            waits=(ev_mean,))
                        ev_var = s2.step(
                            "vector",
                            lambda e: e.tensor_tensor(
                                cols[:, 4:5], cols[:, 6:7], cols[:, 3:4],
                                ALU.subtract),
                            waits=(ev_m2, ev_q))
                        ev_ln = s2.step(
                            "scalar",
                            lambda e: e.activation(cols[:, 7:8], cols[:, 4:5],
                                                   AF.Ln, bias=constc[:, 0:1]),
                            waits=(ev_var,))
                        ev_rstd = s2.step(
                            "scalar",
                            lambda e: e.activation(cols[:, 5:6], cols[:, 7:8],
                                                   AF.Exp, scale=-0.5),
                            waits=(ev_ln,))
                        ev_xn = s2.step(
                            "vector",
                            lambda e, xt=xt, par=par: e.tensor_scalar(
                                xnbf[par][:], xt, cols[:, 2:3], cols[:, 5:6],
                                ALU.subtract, ALU.mult),
                            waits=bw(("xnbf", par), ev_rstd))
                        lastc["cols"] = ev_xn
                        xfree[i] = (ev_xn, ev_sq)
                        if P1A_LEVEL < 1:
                            continue
                        ev_tT = s2.step(
                            "tensor",
                            lambda e, par=par: [
                                e.transpose(pTA[:, c * 128:(c + 1) * 128],
                                            xnbf[par][:, c * 128:(c + 1) * 128],
                                            id_bf[:]) for c in range(8)][-1],
                            waits=bw("pTA", ev_xn))
                        lastc[("xnbf", par)] = ev_tT
                        ev_xnT = s2.step(
                            "vector",
                            lambda e, g=g, tt=tt: e.tensor_copy(
                                xnT[:].rearrange("p (c t) -> p c t",
                                                        t=512)
                                [:, :, tt * 128:tt * 128 + 128],
                                pTA[:].rearrange("p (c t) -> p c t", t=128)),
                            waits=bw("xnTb", ev_tT))
                        lastc["pTA"] = ev_xnT
                        ev_xnT_t[tt] = ev_xnT
                        if P1A_LEVEL < 2:
                            continue
                        def mm_small(e, g=g, tt=tt):
                            for c in range(8):
                                e.matmul(pSm[:, 0:64],
                                         xnT[:, c * 512 + tt * 128:
                                                    c * 512 + tt * 128 + 128],
                                         w_small[:, c * 64:(c + 1) * 64],
                                         start=(c == 0), stop=False)
                            return e.matmul(pSm[:, 0:64], ones_bf[0:1, :],
                                            b_small[:], start=False, stop=True)
                        ev_sm = s2.step("tensor", mm_small,
                                        waits=bw("pSm", ev_xnT, evw))
                        ev_smf = s2.step(
                            "vector",
                            lambda e: e.tensor_copy(smallf[:], pSm[:, 0:64]),
                            waits=bw("smallf", ev_sm))
                        ev_gt = None if own else s2.step(
                            "tensor",
                            lambda e: (e.transpose(pSm[0:4, 64:192],
                                                   smallf[:, 0:4],
                                                   id_f32[:]),
                                       e.transpose(pSm[0:4, 192:320],
                                                   smallf[:, 4:8],
                                                   id_f32[:]))[1],
                            waits=(ev_smf,))
                        if not own:
                            ev_gpre = s2.step(
                                "vector",
                                lambda e, i=i: e.tensor_copy(
                                    gpre[0:4, i * 128:(i + 1) * 128],
                                    pSm[0:4, 64:192]),
                                waits=(ev_gt,))
                            ev_gpre2 = s2.step(
                                "vector",
                                lambda e, i=i: e.tensor_copy(
                                    gpre[32:36, i * 128:(i + 1) * 128],
                                    pSm[0:4, 192:320]),
                                waits=(ev_gt,))
                            lastc["pSm"] = ev_gpre2
                        else:
                            lastc["pSm"] = ev_smf
                        if P1A_LEVEL < 3:
                            continue
                        def NS(j, w=1):
                            return strided(smallf[:], 8 + j, [[16, 3], [1, w]])

                        def R(j, w=1):
                            return strided(rt[:], j, [[10, 3], [1, w]])

                        def RB(j, n):
                            return strided(rt[:], j, [[10, 3], [0, n]])
                        OH8 = strided(rt[:], 30, [[8, 3], [1, 8]])
                        chainops = [
                            lambda e: e.tensor_single_scalar(
                                R(0), NS(0), 0.0, ALU.is_ge),
                            lambda e: e.tensor_tensor(
                                R(1), NS(2), NS(1), ALU.subtract),
                            lambda e: e.tensor_tensor(
                                R(2), R(1), R(0), ALU.mult),
                            lambda e: e.tensor_tensor(
                                R(2), R(2), NS(1), ALU.add),
                            lambda e: e.tensor_single_scalar(
                                R(3), R(2), 0.0, ALU.is_ge),
                            lambda e: e.tensor_scalar(
                                R(4), R(0), 2.0, None, ALU.mult),
                            lambda e: e.tensor_tensor(
                                R(4), R(4), R(3), ALU.add),
                            lambda e: e.tensor_tensor(
                                R(5, 4),
                                iota4[:].unsqueeze(1).broadcast_to((128, 3, 4)),
                                RB(4, 4), ALU.is_equal),
                            lambda e: e.tensor_tensor(
                                R(5, 4), R(5, 4), NS(3, 4), ALU.mult),
                            lambda e: e.tensor_reduce(
                                R(9), R(5, 4), AX.X, ALU.add),
                            lambda e: e.tensor_single_scalar(
                                R(3), R(9), 0.0, ALU.is_ge),
                            lambda e: e.tensor_scalar(
                                R(4), R(4), 2.0, None, ALU.mult),
                            lambda e: e.tensor_tensor(
                                R(4), R(4), R(3), ALU.add),
                            lambda e: e.tensor_tensor(
                                OH8,
                                iota16[:, 0:8].unsqueeze(1).broadcast_to(
                                    (128, 3, 8)),
                                RB(4, 8), ALU.is_equal),
                            lambda e: e.tensor_tensor(
                                OH8, OH8, NS(7, 8), ALU.mult),
                            lambda e: e.tensor_reduce(
                                R(9), OH8, AX.X, ALU.add),
                            lambda e: e.tensor_single_scalar(
                                R(3), R(9), 0.0, ALU.is_ge),
                            lambda e: e.tensor_scalar(
                                R(4), R(4), 2.0, None, ALU.mult),
                            lambda e: e.tensor_tensor(
                                R(4), R(4), R(3), ALU.add),
                            lambda e: e.tensor_tensor(
                                sel[:].rearrange("p (c f) -> p c f", f=16),
                                iota16[:].unsqueeze(1).broadcast_to(
                                    (128, 3, 16)),
                                RB(4, 16), ALU.is_equal),
                            lambda e: e.tensor_copy(selbf[:], sel[:]),
                        ]
                        prev = ev_smf
                        for op in chainops:
                            prev = s2.step("vector", op, waits=(prev,))
                        ev_sel = prev
                        lastc["smallf"] = ev_sel
                        sel_src = selbf[:, 0:16] if own else None

                        def t_selT(e):
                            if own:
                                return e.transpose(pTA[0:16, 0:128],
                                                   selbf[:, 0:16], id_bf[:])
                            e.transpose(pTA[0:16, 0:128], selbf[:, 32:48],
                                        id_bf[:])
                            return e.transpose(pTA[0:16, 128:256],
                                               selbf[:, 16:32], id_bf[:])
                        ev_tsT = s2.step("tensor", t_selT,
                                         waits=bw("pTA", ev_sel))
                        ev_selT = s2.step(
                            "vector",
                            lambda e: e.tensor_copy(selT[:], pTA[0:16, 0:256]),
                            waits=bw("selT", ev_tsT))
                        lastc["pTA"] = ev_selT
                        if P1A_LEVEL < 4:
                            continue
                        ev_kTsrc = None
                        for fi, f in enumerate(ffs):
                            if P1A_LEVEL < 5 and fi > 0:
                                continue
                            hb = (pB[0], pB[1]) if fi % 2 == 0 else \
                                (pB[2], pB[3])
                            ob = (pB[2], pB[3]) if fi % 2 == 0 else \
                                (pB[0], pB[1])
                            hkey = "pB01" if fi % 2 == 0 else "pB23"
                            okey = "pB23" if fi % 2 == 0 else "pB01"

                            def mm_hid(e, f=f, g=g, tt=tt, hb=hb):
                                for c in range(8):
                                    for h2 in range(2):
                                        e.matmul(
                                            hb[h2][:],
                                            xnT[:,
                                                       c * 512 + tt * 128:
                                                       c * 512 + tt * 128
                                                       + 128],
                                            wf1[f][:,
                                                   c * 1024 + h2 * 512:
                                                   c * 1024 + (h2 + 1) * 512],
                                            start=(c == 0), stop=False)
                                e.matmul(hb[0][:], ones_bf[0:1, :],
                                         bf1[f][0:1, 0:512], start=False,
                                         stop=True)
                                return e.matmul(hb[1][:], ones_bf[0:1, :],
                                                bf1[f][0:1, 512:1024],
                                                start=False, stop=True)
                            ev_hid = s2.step("tensor", mm_hid,
                                             waits=bw(hkey, ev_xnT, evw))
                            ev_rl0 = s2.step(
                                "scalar",
                                lambda e, hb=hb: e.activation(
                                    hrelu[:, 0:512], hb[0][:], AF.Relu),
                                waits=bw("hrelu", ev_hid))
                            ev_rl1 = s2.step(
                                "scalar",
                                lambda e, hb=hb: e.activation(
                                    hrelu[:, 512:1024], hb[1][:], AF.Relu),
                                waits=(ev_hid,))
                            lastc[hkey] = ev_rl1
                            fidx = {"q": 0, "k": 1, "v": 2}[f]

                            def selbc(lo, n, fidx=fidx):
                                return strided(selbf[:], fidx * 16 + lo,
                                               [[1, n], [0, 64]])
                            ev_mk0 = s2.step(
                                "vector",
                                lambda e, selbc=selbc: e.tensor_tensor(
                                    hidbf[:].rearrange("p (c f) -> p c f",
                                                       f=64)[:, 0:8],
                                    hrelu[:].rearrange("p (c f) -> p c f",
                                                       f=64)[:, 0:8],
                                    selbc(0, 8), ALU.mult),
                                waits=bw("hidbf", ev_rl0, ev_sel))
                            ev_mk1 = s2.step(
                                "vector",
                                lambda e, selbc=selbc: e.tensor_tensor(
                                    hidbf[:].rearrange("p (c f) -> p c f",
                                                       f=64)[:, 8:16],
                                    hrelu[:].rearrange("p (c f) -> p c f",
                                                       f=64)[:, 8:16],
                                    selbc(8, 8), ALU.mult),
                                waits=(ev_rl1,))
                            lastc["hrelu"] = ev_mk1
                            ev_thT = s2.step(
                                "tensor",
                                lambda e: [
                                    e.transpose(pTA[:, c * 128:(c + 1) * 128],
                                                hidbf[:,
                                                      c * 128:(c + 1) * 128],
                                                id_bf[:])
                                    for c in range(8)][-1],
                                waits=bw("pTA", ev_mk0, ev_mk1))
                            lastc["hidbf"] = ev_thT
                            ev_hT = s2.step(
                                "vector",
                                lambda e: e.tensor_copy(hidT[:], pTA[:]),
                                waits=bw("hidT", ev_thT))
                            lastc["pTA"] = ev_hT

                            def mm_out(e, f=f, ob=ob):
                                for c in range(8):
                                    for h2 in range(2):
                                        e.matmul(
                                            ob[h2][:],
                                            hidT[:, c * 128:(c + 1) * 128],
                                            wf2[f][:,
                                                   c * 1024 + h2 * 512:
                                                   c * 1024 + (h2 + 1) * 512],
                                            start=(c == 0), stop=False)
                                sT = selT[0:16, 0:128] if (f == "v" or own) \
                                    else selT[0:16, 128:256]
                                e.matmul(ob[0][:], sT, bf2[f][:, 0:512],
                                         start=False, stop=True)
                                return e.matmul(ob[1][:], sT,
                                                bf2[f][:, 512:1024],
                                                start=False, stop=True)
                            ev_out = s2.step("tensor", mm_out,
                                             waits=bw(okey, ev_hT, ev_selT))
                            lastc["hidT"] = ev_out
                            lastc["selT"] = ev_out
                            dst = (outv if f == "v" else outk)[par]
                            ev_oe0 = s2.step(
                                "vector",
                                lambda e, dst=dst, ob=ob: e.tensor_copy(
                                    dst[:, 0:512], ob[0][:]),
                                waits=bw(("out", f, par), ev_out))
                            ev_oe1 = s2.step(
                                "vector",
                                lambda e, dst=dst, ob=ob: e.tensor_copy(
                                    dst[:, 512:1024], ob[1][:]),
                                waits=(ev_out,))
                            lastc[okey] = ev_oe1
                            if f == "v":
                                evst = D_(s2, s_ffnv[i * 128:(i + 1) * 128, :],
                                          outv[par][:], "scalar",
                                          waits=(ev_oe0, ev_oe1),
                                          slot=f"sv{par}")
                                lastc[("out", "v", par)] = evst
                            else:
                                ev_kTsrc = (ev_oe0, ev_oe1)
                        if P1A_LEVEL < 5 and not own:
                            continue
                        ev_tk = s2.step(
                            "tensor",
                            lambda e, par=par: [
                                e.transpose(pTA[:, c * 128:(c + 1) * 128],
                                            outk[par][:, c * 128:(c + 1) * 128],
                                            id_bf[:]) for c in range(8)][-1],
                            waits=bw("pTA", *ev_kTsrc))
                        ev_kT = s2.step(
                            "vector",
                            lambda e, par=par: e.tensor_copy(kT[par][:],
                                                             pTA[:]),
                            waits=bw(("kT", par), ev_tk))
                        lastc["pTA"] = ev_kT
                        fT = "q" if own else "k"
                        lastc[("out", fT, par)] = ev_tk
                        if not own:
                            ev_ks = s2.step(
                                "tensor",
                                lambda e, par=par, i=i: [
                                    e.matmul(pKs[h2][:], ones_bf[:, 0:2],
                                             outk[par][:,
                                                       h2 * 512:
                                                       (h2 + 1) * 512],
                                             start=(i == 0),
                                             stop=(i == NT - 1))
                                    for h2 in range(2)][-1],
                                waits=ev_kTsrc)
                            lastc[("out", "k", par)] = ev_ks
                            ks_last = ev_ks
                        dscr = s_ffnkT if not own else s_ffnqT
                        evst = D_(s2,
                                  dscr.rearrange("(c p) t -> p c t", p=128)
                                  [:, :, i * 128:i * 128 + 128],
                                  kT[par][:].rearrange("p (c t) -> p c t",
                                                       t=128),
                                  "scalar", waits=(ev_kT,), slot=f"sk{par}")
                        lastc[("kT", par)] = evst

                    if P1A_LEVEL < 6:
                        continue
                    if own:
                        lastc["xnTb"] = ev_hid
                        continue
                    exT = ev_xnT_t[3]
                    last_grp = None
                    for wsrc, dst_scr in [(w_q, s_mixqT), (w_k, s_mixkT)]:
                        for mc in range(2):
                            def mm_p(e, wsrc=wsrc, mc=mc, g=g):
                                last = None
                                for c in range(8):
                                    last = e.matmul(
                                        pB[mc][:],
                                        wsrc[:, c * 256 + mc * 128:
                                             c * 256 + (mc + 1) * 128],
                                        xnT[:, c * 512:(c + 1) * 512],
                                        start=(c == 0), stop=(c == 7))
                                return last
                            evp = s2.step("tensor", mm_p,
                                          waits=bw(f"gB{mc}", exT,
                                                   lastc.get("pB01")))
                            eve = s2.step(
                                "vector",
                                lambda e, mc=mc: e.tensor_copy(
                                    projbf[mc][:], pB[mc][:]),
                                waits=bw(("projbf", mc), evp))
                            lastc[f"gB{mc}"] = eve
                            evs = D_(s2,
                                     dst_scr[mc * 128:(mc + 1) * 128,
                                             g * 512:(g + 1) * 512],
                                     projbf[mc][:], "scalar", waits=(eve,),
                                     slot="sp")
                            lastc[("projbf", mc)] = evs
                            last_grp = eve
                    lastc["pB01"] = None
                    for tt in range(4 if P1A_LEVEL >= 7 else 0):
                        bank = pB[2 + tt % 2]

                        def mm_v(e, g=g, tt=tt, bank=bank):
                            last = None
                            for c in range(8):
                                last = e.matmul(
                                    bank[:],
                                    xnT[:, c * 512 + tt * 128:
                                               c * 512 + tt * 128 + 128],
                                    w_v[:, c * 512:(c + 1) * 512],
                                    start=(c == 0), stop=(c == 7))
                            return last
                        evp = s2.step(
                            "tensor", mm_v,
                            waits=bw(f"gV{tt % 2}", exT,
                                     lastc.get("pB23") if tt == 0 else None))
                        eve = s2.step(
                            "vector",
                            lambda e, tt=tt, bank=bank: e.tensor_copy(
                                projbf[2 + tt % 2][:], bank[:]),
                            waits=bw(("projbf", 2 + tt % 2), evp))
                        lastc[f"gV{tt % 2}"] = eve
                        evs = D_(s2,
                                 s_mixv[:, (4 * g + tt) * 128:
                                        (4 * g + tt) * 128 + 128, :]
                                 .rearrange("h t d -> t h d"),
                                 projbf[2 + tt % 2][:].rearrange(
                                     "p (h d) -> p h d", d=128),
                                 "scalar", waits=(eve,), slot="smv")
                        lastc[("projbf", 2 + tt % 2)] = evs
                    lastc["pB23"] = lastc.get("gV1")
                    for mc in range(4 if P1A_LEVEL >= 8 else 0):
                        bank = pB[mc % 2]

                        def mm_o(e, g=g, mc=mc, bank=bank):
                            last = None
                            for c in range(8):
                                last = e.matmul(
                                    bank[:],
                                    w_o[:, c * 512 + mc * 128:
                                        c * 512 + (mc + 1) * 128],
                                    xnT[:, c * 512:(c + 1) * 512],
                                    start=(c == 0), stop=(c == 7))
                            return last
                        evp = s2.step(
                            "tensor", mm_o,
                            waits=bw(f"gO{mc % 2}", exT,
                                     lastc.get(f"gB{mc % 2}") if mc < 2
                                     else None))
                        evt = s2.step(
                            "scalar",
                            lambda e, mc=mc, bank=bank: e.activation(
                                sigf[mc % 2][:], bank[:], AF.Tanh, scale=0.5),
                            waits=bw(("sigf", mc % 2), evp))
                        lastc[f"gO{mc % 2}"] = evt
                        evsg = s2.step(
                            "vector",
                            lambda e, mc=mc: e.tensor_scalar(
                                sigbf[mc][:], sigf[mc % 2][:], 0.5, 0.5,
                                ALU.mult, ALU.add),
                            waits=bw(("sigbf", mc), evt))
                        lastc[("sigf", mc % 2)] = evsg
                        evs = D_(s2,
                                 s_sigoT[mc * 128:(mc + 1) * 128,
                                         g * 512:(g + 1) * 512],
                                 sigbf[mc][:], "scalar", waits=(evsg,),
                                 slot="ssg")
                        lastc[("sigbf", mc)] = evs
                    lastc["pB01"] = lastc.get("gO1")
                    lastc["xnTb"] = lastc.get("gO1")

                if not own and P1A_LEVEL >= 5:
                    ev_kse = s2.step(
                        "vector",
                        lambda e: e.tensor_copy(ksum_sb[0:2, 0:512],
                                                pKs[0][:]),
                        waits=(ks_last,))
                    ev_kse2 = s2.step(
                        "vector",
                        lambda e: e.tensor_copy(ksum_sb[0:2, 512:1024],
                                                pKs[1][:]),
                        waits=(ks_last,))
                    if debug:
                        D_(s2, dbgout("ksum", (1, D)), ksum_sb[0:1, :],
                           "scalar", waits=(ev_kse, ev_kse2), slot="dbg")
                        D_(s2, dbgout("gprei", (4, NT * 128)),
                           gpre[0:4, 0:NT * 128], "scalar", slot="dbg")
                        D_(s2, dbgout("gpref", (4, NT * 128)),
                           gpre[32:36, 0:NT * 128], "scalar", slot="dbg")

                with nc.Block("p1" + name) as blk:
                    s2.emit(blk)

        if "P1A" in active:
            production_phase(False)
        if "P1B" in active:
            production_phase(True)

        NBq = NT // 4
        if "P15" in active:
            P5 = ExitStack()
            with P5:
                sb5 = lambda n, sh, dt: P5.enter_context(
                    nc.sbuf_tensor(n, list(sh), dt))
                CH = min(2048, NT * 128)
                NCH = (NT * 128) // CH
                gA = sb5("gA", (4, 2 * CH), F32)
                gB = sb5("gB", (4, CH), F32)
                gC = sb5("gC", (4, CH), F32)
                gD = sb5("gD", (4, CH), F32)
                sc12 = sb5("sc12", (12, NT * 128), F32)
                carry = sb5("carry", (4, 4), F32)
                cvrow = sb5("cvrow", (4, 16), F32)
                cvr1 = sb5("cvr1", (1, 64), F32)
                cvb = sb5("cvb", (1, 64), BF16)
                pP5 = P5.enter_context(nc.psum_tensor("pP5", [128, 64], F32))

                s5 = Sched(sems, makesem)
                prev = {}
                for ch in range(NCH):
                    cs_ = slice(ch * CH, (ch + 1) * CH)
                    e1 = s5.step(
                        "scalar",
                        lambda e, cs_=cs_: (
                            e.activation(gA[:, 0:CH], gpre[0:4, cs_],
                                         AF.Tanh, scale=1.0 / 15.0),
                            e.activation(gA[:, CH:2 * CH], gpre[32:36, cs_],
                                         AF.Tanh, scale=1.0 / 15.0))[1],
                        waits=(prev.get("M"), prev.get("a"),
                               prev.get("acomp")))
                    e2 = s5.step(
                        "vector",
                        lambda e: e.tensor_scalar(gA[:], gA[:], 15.0, None,
                                                  ALU.mult),
                        waits=(e1,))
                    e3 = s5.step(
                        "scalar",
                        lambda e: e.activation(gB[:], gA[:, CH:2 * CH],
                                               AF.Abs),
                        waits=(e2, prev.get("M")))
                    e4 = s5.step(
                        "scalar",
                        lambda e: e.activation(gB[:], gB[:], AF.Exp,
                                               scale=-1.0),
                        waits=(e3,))
                    e5 = s5.step(
                        "scalar",
                        lambda e: e.activation(gB[:], gB[:], AF.Ln,
                                               bias=constc[0:4, 3:4]),
                        waits=(e4,))
                    e6 = s5.step(
                        "vector",
                        lambda e: e.tensor_single_scalar(
                            gC[:], gA[:, CH:2 * CH], 0.0, ALU.min),
                        waits=(e2, prev.get("a"), prev.get("Mcomp")))
                    e7 = s5.step(
                        "vector",
                        lambda e: e.tensor_tensor(gC[:], gC[:], gB[:],
                                                  ALU.subtract),
                        waits=(e5, e6))
                    e8 = s5.step(
                        "vector",
                        lambda e, ch=ch: e.tensor_tensor_scan(
                            gD[:], gC[:], gC[:],
                            0.0 if ch == 0 else carry[0:4, 0:1],
                            ALU.add, ALU.bypass),
                        waits=(e7, prev.get("acomp"), prev.get("lfc"),
                               prev.get("cscarry")))
                    prev["cscarry"] = s5.step(
                        "vector",
                        lambda e: e.tensor_copy(carry[0:4, 0:1],
                                                gD[:, CH - 1:CH]),
                        waits=(e8,))
                    prev["lfc"] = s5.step(
                        "sync",
                        lambda e, cs_=cs_: e.dma_start(
                            out=sc12[4:8, cs_], in_=gD[:]),
                        waits=(e8,), dma="p5l")
                    e9 = s5.step(
                        "vector",
                        lambda e: e.tensor_tensor(gC[:], gA[:, 0:CH], gD[:],
                                                  ALU.subtract),
                        waits=(e8,))
                    prev["acomp"] = e9
                    prev["a"] = s5.step(
                        "sync",
                        lambda e, cs_=cs_: e.dma_start(
                            out=sc12[0:4, cs_], in_=gC[:]),
                        waits=(e9,), dma="p5a")
                    e10 = s5.step(
                        "vector",
                        lambda e, ch=ch: e.tensor_tensor_scan(
                            gB[:], gC[:], gC[:],
                            -1e30 if ch == 0 else carry[0:4, 1:2],
                            ALU.max, ALU.bypass),
                        waits=(e9, e5, prev.get("cmcarry")))
                    prev["Mcomp"] = e10
                    prev["cmcarry"] = s5.step(
                        "vector",
                        lambda e: e.tensor_copy(carry[0:4, 1:2],
                                                gB[:, CH - 1:CH]),
                        waits=(e10,))
                    prev["M"] = s5.step(
                        "sync",
                        lambda e, cs_=cs_: e.dma_start(
                            out=sc12[8:12, cs_], in_=gB[:]),
                        waits=(e10,), dma="p5m")
                evt_prev = None
                ev_gcol = None
                for g4 in range(0, NT, 4):
                    def t_g(e, g4=g4):
                        last = None
                        for j in range(4):
                            last = e.transpose(
                                pP5[:, j * 12:j * 12 + 12],
                                sc12[:, (g4 + j) * 128:(g4 + j + 1) * 128],
                                id_f32[0:12, 0:12])
                        return last
                    evt = s5.step("tensor", t_g,
                                  waits=(prev["M"], prev["a"], prev["lfc"],
                                         evt_prev))
                    evt_prev = s5.step(
                        "vector",
                        lambda e, g4=g4: e.tensor_copy(
                            gcol[:, g4 * 12:(g4 + 4) * 12], pP5[:, 0:48]),
                        waits=(evt,))
                ev_gcol = evt_prev
                evc0 = s5.step("vector",
                               lambda e: e.memset(cvr1[:], 0.0))
                evc1 = s5.step(
                    "sync",
                    lambda e: e.dma_start(
                        out=cvrow[0:4, 0:NBq],
                        in_=strided(sc12[8:12, :], 511, [[512, NBq]]),
                        allow_slow_non_contiguous=True),
                    waits=(prev["M"],), dma="p5c")
                evc2 = s5.step(
                    "sync",
                    lambda e: e.dma_start(
                        out=strided(cvr1[0:1, :], 0, [[16, 4], [1, NBq]]),
                        in_=cvrow[0:4, 0:NBq],
                        allow_slow_non_contiguous=True),
                    waits=(evc1, evc0), dma="p5c2")
                evc3 = s5.step("vector",
                               lambda e: e.tensor_copy(cvb[:], cvr1[:]),
                               waits=(evc2,))
                evc4 = s5.step(
                    "tensor",
                    lambda e: e.matmul(pP5[0:128, 0:64], ones_bf[0:1, :],
                                       cvb[0:1, 0:64], start=True,
                                       stop=True),
                    waits=(evc3, ev_gcol))
                s5.step("vector",
                        lambda e: e.tensor_copy(crep[:], pP5[:, 0:64]),
                        waits=(evc4,))
                if debug:
                    evcrep = s5.steps["vector"][-1].ev
                    D_(s5, dbgout("sc12", (12, NT * 128)), sc12[:], "scalar",
                       waits=(prev["M"], prev["a"], prev["lfc"]), slot="dbg")
                    D_(s5, dbgout("gcol", (128, NT * 12)),
                       gcol[:, 0:NT * 12], "scalar", waits=(ev_gcol,),
                       slot="dbg")
                    D_(s5, dbgout("crep", (128, 64)), crep[:], "scalar",
                       waits=(evcrep,), slot="dbg")
                with nc.Block("p15") as blk:
                    s5.emit(blk)

        if "P2" in active:
            P6 = ExitStack()
            with P6:
                sb6 = lambda n, sh, dt: P6.enter_context(
                    nc.sbuf_tensor(n, list(sh), dt))
                NTT = NT
                kqT = sb6("kqT", (128, NTT * 128), BF16)
                qqT = sb6("qqT", (128, NTT * 128), BF16)
                vau = sb6("vau", (128, 2 * NTT * 132), BF16)
                sTb = [sb6(f"sTb{i}", (128, 512), BF16) for i in range(4)]
                csb = sb6("csb", (128, 2 * NTT), F32)
                rwf = sb6("rwf", (128, 32), F32)
                hf8 = sb6("hf8", (128, 1024), F32)
                st8 = sb6("st8", (128, 32), F32)
                tr6 = sb6("tr6", (128, 1024), F32)
                hnb = sb6("hnb", (128, 128), BF16)
                sgb = [sb6(f"sgb{i}", (128, 2 * 512), BF16) for i in range(2)]
                gat = sb6("gat", (128, 2 * 512), BF16)

                pS = [P6.enter_context(nc.psum_tensor(f"pS{i}", [128, 512],
                                                      F32)) for i in range(4)]
                pA_ = [P6.enter_context(nc.psum_tensor(f"pA{i}", [128, 512],
                                                       F32))
                       for i in range(3)]
                pT6 = P6.enter_context(nc.psum_tensor("pT6", [128, 128],
                                                      BF16))
                REG = [(pA_[r // 3], (r % 3) * 132) for r in range(8)]

                s6 = Sched(sems, makesem)
                evcoll = []
                ev_last_score = None
                ev_last_vmm = None
                ev_hfin_last = None
                ev_lastnorm = None
                stores_all = []
                sg_guard = {}
                ev_prep_guard = None
                for m in range(2):
                    ld_w = (ev_last_score, ev_last_vmm)
                    evk = D_(s6, kqT[:],
                             s_mixkT[128 * m:128 * (m + 1), 0:NTT * 128],
                             "sync", waits=ld_w, slot="mk")
                    evq = D_(s6, qqT[:],
                             s_mixqT[128 * m:128 * (m + 1), 0:NTT * 128],
                             "sync", waits=ld_w, slot="mq")
                    evv = []
                    for hh in range(2):
                        evv.append(D_(
                            s6,
                            strided(vau[:], hh * NTT * 132,
                                    [[132, NTT], [1, 128]]),
                            s_mixv[2 * m + hh, 0:NTT * 128, :].rearrange(
                                "(t p) d -> p t d", p=128),
                            "sync", waits=ld_w, slot=f"mv{hh}"))
                    evones = s6.step(
                        "vector",
                        lambda e: e.memset(
                            strided(vau[:], 128, [[132, 2 * NTT], [1, 1]]),
                            1.0),
                        waits=tuple(evv) + (ev_last_vmm,))
                    last_evac = {0: None, 1: None}
                    for Bq in range(NBq):
                        ntau = 4 * Bq + 4
                        ev_sg = D_(s6, sgb[Bq % 2][:].rearrange(
                                       "p (h t) -> p h t", t=512),
                                   s_sigoT[256 * m:256 * (m + 1),
                                           Bq * 512:(Bq + 1) * 512]
                                   .rearrange("(h p) t -> p h t", p=128),
                                   "sync",
                                   waits=(sg_guard.get(Bq % 2),),
                                   slot=f"sg{Bq % 2}")
                        aview = strided(gcol[:], 2 * m, [[12, ntau], [1, 2]])
                        crepb_t = strided(crep[:], 16 * 2 * m + Bq,
                                          [[0, ntau], [16, 2]])
                        csv = strided(csb[:], 0, [[2, ntau], [1, 2]])
                        ep1 = s6.step(
                            "vector",
                            lambda e, aview=aview, crepb_t=crepb_t, csv=csv:
                            e.tensor_tensor(csv, aview, crepb_t,
                                            ALU.subtract),
                            waits=(ev_prep_guard,))
                        ep2 = s6.step(
                            "vector",
                            lambda e, ntau=ntau: e.tensor_scalar(
                                csb[:, 0:2 * ntau], csb[:, 0:2 * ntau],
                                80.0, -2.0794415416798357,
                                ALU.min, ALU.add),
                            waits=(ep1,))
                        ep3 = s6.step(
                            "scalar",
                            lambda e, ntau=ntau: e.activation(
                                csb[:, 0:2 * ntau], csb[:, 0:2 * ntau],
                                AF.Exp),
                            waits=(ep2,))
                        Mv = strided(gcol[:], 4 * Bq * 12 + 8 + 2 * m,
                                     [[1, 2], [12, 4]])
                        lv = strided(gcol[:], 4 * Bq * 12 + 4 + 2 * m,
                                     [[1, 2], [12, 4]])
                        crepb_r = strided(crep[:], 16 * 2 * m + Bq,
                                          [[16, 2], [0, 4]])
                        rv = strided(rwf[:], 0, [[4, 2], [1, 4]])
                        fv = strided(rwf[:], 8, [[4, 2], [1, 4]])
                        er1 = s6.step(
                            "vector",
                            lambda e, rv=rv, Mv=Mv, crepb_r=crepb_r:
                            e.tensor_tensor(rv, crepb_r, Mv, ALU.subtract),
                            waits=(ev_prep_guard,))
                        er2 = s6.step(
                            "vector",
                            lambda e: e.tensor_single_scalar(
                                rwf[:, 0:8], rwf[:, 0:8], 80.0, ALU.min),
                            waits=(er1,))
                        er3 = s6.step(
                            "scalar",
                            lambda e: e.activation(rwf[:, 0:8], rwf[:, 0:8],
                                                   AF.Exp),
                            waits=(er2,))
                        ef1 = s6.step(
                            "vector",
                            lambda e, fv=fv, Mv=Mv, lv=lv:
                            e.tensor_tensor(fv, lv, Mv, ALU.add),
                            waits=(er1, ev_prep_guard))
                        ef2 = s6.step(
                            "scalar",
                            lambda e: e.activation(rwf[:, 8:16],
                                                   rwf[:, 8:16], AF.Exp,
                                                   scale=-1.0),
                            waits=(ef1,))
                        for tau in range(ntau):
                            par = tau % 2
                            def mm_s(e, m=m, Bq=Bq, tau=tau, par=par):
                                e.matmul(pS[2 * par][:],
                                         kqT[0:64,
                                             tau * 128:(tau + 1) * 128],
                                         qqT[0:64,
                                             Bq * 512:(Bq + 1) * 512],
                                         start=True, stop=True,
                                         tile_position=(0, 0))
                                return e.matmul(
                                    pS[2 * par + 1][:],
                                    kqT[64:128, tau * 128:(tau + 1) * 128],
                                    qqT[64:128, Bq * 512:(Bq + 1) * 512],
                                    start=True, stop=True,
                                    tile_position=(64, 0))
                            ev_s = s6.step(
                                "tensor", mm_s,
                                waits=(evk, evq) + (last_evac[par] or ()))
                            ev_last_score = ev_s
                            ee = []
                            for hh in range(2):
                                cs_col = csb[:, 2 * tau + hh:
                                             2 * tau + hh + 1]
                                if tau >= 4 * Bq:
                                    jj = tau - 4 * Bq
                                    ee.append(s6.step(
                                        "vector",
                                        lambda e, hh=hh, par=par, jj=jj,
                                        cs_col=cs_col:
                                        e.scalar_tensor_tensor(
                                            sTb[2 * par + hh][:],
                                            pS[2 * par + hh][:],
                                            cs_col,
                                            masks[:, jj * 512:
                                                  (jj + 1) * 512],
                                            ALU.mult, ALU.mult),
                                        waits=(ev_s, ep3, ev_last_vmm)))
                                elif hh == 0:
                                    ee.append(s6.step(
                                        "scalar",
                                        lambda e, par=par, cs_col=cs_col:
                                        e.activation(
                                            sTb[2 * par][:],
                                            pS[2 * par][:],
                                            AF.Copy, scale=cs_col),
                                        waits=(ev_s, ep3, ev_last_vmm)))
                                else:
                                    ee.append(s6.step(
                                        "vector",
                                        lambda e, par=par, cs_col=cs_col:
                                        e.tensor_scalar(
                                            sTb[2 * par + 1][:],
                                            pS[2 * par + 1][:],
                                            cs_col, None, ALU.mult),
                                        waits=(ev_s, ep3, ev_last_vmm)))
                            last_evac[par] = tuple(ee)
                            def mm_v6(e, tau=tau, par=par, ntau=ntau):
                                last = None
                                for hh in range(2):
                                    for j in range(4):
                                        r = hh * 4 + j
                                        bank, off = REG[r]
                                        last = e.matmul(
                                            bank[:, off:off + 129],
                                            sTb[2 * par + hh][
                                                :, j * 128:(j + 1) * 128],
                                            strided(vau[:],
                                                    (hh * NTT + tau) * 132,
                                                    [[1, 129]]),
                                            start=(tau == 0 and r % 3 == 0),
                                            stop=(tau == ntau - 1 and
                                                  r in (2, 5, 7)))
                                return last
                            w6 = last_evac[par] + (evones,)
                            if tau == 0:
                                w6 = w6 + (ev_hfin_last,)
                            ev_last_vmm = s6.step("tensor", mm_v6, waits=w6)
                        evV = ev_last_vmm
                        eg = []
                        for bnk in range(3):
                            nreg = 3 if bnk < 2 else 2
                            eg.append(s6.step(
                                "vector",
                                lambda e, bnk=bnk, nreg=nreg:
                                e.tensor_copy(
                                    rwf[:, 16 + 3 * bnk:
                                        16 + 3 * bnk + nreg],
                                    strided(pA_[bnk][:], 128,
                                            [[132, nreg]])),
                                waits=(evV,)))
                        en1 = s6.step(
                            "vector",
                            lambda e: e.tensor_tensor(
                                rwf[:, 16:24], rwf[:, 16:24], rwf[:, 0:8],
                                ALU.mult),
                            waits=tuple(eg) + (er3,))
                        en2 = s6.step(
                            "vector",
                            lambda e: e.tensor_scalar(
                                rwf[:, 24:32], rwf[:, 16:24], -1.0, None,
                                ALU.mult),
                            waits=(en1,))
                        en3 = s6.step(
                            "vector",
                            lambda e: e.tensor_tensor(
                                rwf[:, 16:24], rwf[:, 16:24],
                                rwf[:, 24:32], ALU.max),
                            waits=(en2,))
                        en4 = s6.step(
                            "vector",
                            lambda e: e.tensor_tensor(
                                rwf[:, 16:24], rwf[:, 16:24], rwf[:, 8:16],
                                ALU.max),
                            waits=(en3, ef2))
                        en5 = s6.step(
                            "vector",
                            lambda e: e.tensor_scalar(
                                rwf[:, 16:24], rwf[:, 16:24], 1e-6, None,
                                ALU.add),
                            waits=(en4,))
                        en6 = s6.step(
                            "vector",
                            lambda e: e.reciprocal(rwf[:, 24:32],
                                                   rwf[:, 16:24]),
                            waits=(en5,))
                        en7 = s6.step(
                            "vector",
                            lambda e: e.tensor_tensor(
                                rwf[:, 24:32], rwf[:, 24:32], rwf[:, 0:8],
                                ALU.mult),
                            waits=(en6, er3))
                        ehf = []
                        for r in range(8):
                            bank, off = REG[r]
                            ehf.append(s6.step(
                                "vector",
                                lambda e, bank=bank, off=off, r=r:
                                e.tensor_scalar(
                                    hf8[:, r * 128:(r + 1) * 128],
                                    bank[:, off:off + 128],
                                    rwf[:, 24 + r:25 + r], None, ALU.mult),
                                waits=(en7, ev_lastnorm)))
                        ev_hfin_last = ehf[-1]
                        es1 = s6.step(
                            "vector",
                            lambda e: e.reduce_sum(
                                st8[:, 0:8],
                                hf8[:].rearrange("p (r t) -> p r t", t=128),
                                AX.X),
                            waits=tuple(ehf))
                        es2 = s6.step(
                            "scalar",
                            lambda e: e.activation(tr6[:], hf8[:],
                                                   AF.Square),
                            waits=tuple(ehf))
                        es3 = s6.step(
                            "vector",
                            lambda e: e.reduce_sum(
                                st8[:, 8:16],
                                tr6[:].rearrange("p (r t) -> p r t", t=128),
                                AX.X),
                            waits=(es2,))
                        es4 = s6.step(
                            "vector",
                            lambda e: e.tensor_scalar(
                                st8[:, 16:24], st8[:, 0:8], 1.0 / 128,
                                None, ALU.mult),
                            waits=(es1,))
                        es5 = s6.step(
                            "vector",
                            lambda e: e.tensor_tensor(
                                st8[:, 0:8], st8[:, 16:24], st8[:, 16:24],
                                ALU.mult),
                            waits=(es4,))
                        es6 = s6.step(
                            "vector",
                            lambda e: e.tensor_scalar(
                                st8[:, 8:16], st8[:, 8:16], 1.0 / 128,
                                None, ALU.mult),
                            waits=(es3,))
                        es7 = s6.step(
                            "vector",
                            lambda e: e.tensor_tensor(
                                st8[:, 8:16], st8[:, 8:16], st8[:, 0:8],
                                ALU.subtract),
                            waits=(es5, es6))
                        es8 = s6.step(
                            "scalar",
                            lambda e: e.activation(st8[:, 8:16],
                                                   st8[:, 8:16], AF.Ln,
                                                   bias=constc[:, 1:2]),
                            waits=(es7,))
                        es9 = s6.step(
                            "scalar",
                            lambda e: e.activation(st8[:, 24:32],
                                                   st8[:, 8:16], AF.Exp,
                                                   scale=-0.5),
                            waits=(es8,))
                        evfin = []
                        evtp = None
                        for r in range(8):
                            hh, j = r // 4, r % 4
                            eh1 = s6.step(
                                "vector",
                                lambda e, r=r: e.tensor_scalar(
                                    hnb[:], hf8[:, r * 128:(r + 1) * 128],
                                    st8[:, 16 + r:17 + r],
                                    st8[:, 24 + r:25 + r],
                                    ALU.subtract, ALU.mult),
                                waits=(es9, evtp))
                            eh2 = s6.step(
                                "tensor",
                                lambda e: e.transpose(pT6[:], hnb[:],
                                                      id_bf[:]),
                                waits=(eh1, evfin[-1] if evfin else None))
                            evtp = eh2
                            eh3 = s6.step(
                                "vector",
                                lambda e, m=m, hh=hh, j=j, r=r:
                                e.tensor_scalar(
                                    gat[:, r * 128:(r + 1) * 128],
                                    pT6[:],
                                    onorm[:, 2 * m + hh:2 * m + hh + 1],
                                    None, ALU.mult),
                                waits=(eh2, stores_all[-1] if stores_all
                                       else None))
                            eh4 = s6.step(
                                "vector",
                                lambda e, Bq=Bq, r=r: e.tensor_tensor(
                                    gat[:, r * 128:(r + 1) * 128],
                                    gat[:, r * 128:(r + 1) * 128],
                                    sgb[Bq % 2][:, r * 128:(r + 1) * 128],
                                    ALU.mult),
                                waits=(eh3, ev_sg))
                            evfin.append(eh4)
                        ev_lastnorm = eh1
                        sg_guard[Bq % 2] = evfin[-1]
                        evst6 = D_(
                            s6,
                            s_gatedT[256 * m:256 * (m + 1),
                                     Bq * 512:(Bq + 1) * 512]
                            .rearrange("(h p) t -> p h t", p=128),
                            gat[:].rearrange("p (h t) -> p h t", t=512),
                            "scalar", waits=tuple(evfin), slot="sgt")
                        stores_all.append(evst6)
                        ev_prep_guard = evst6
                    for hh in range(2 if NT == 64 else 0):
                        oh = 2 * m + hh
                        evcoll.append(s6.step(
                            "gpsimd",
                            lambda e, oh=oh: e.collective_compute(
                                "AllGather", ALU.bypass,
                                replica_groups=[list(range(8))],
                                ins=[s_gatedT[oh * 128:(oh + 1) * 128, :]
                                     .opt()],
                                outs=[g_gath[oh].opt()]),
                            waits=(stores_all[-1],),
                            coll=f"ag{oh}"))
                if debug:
                    D_(s6, dbgout("hf8", (128, 1024)), hf8[:], "scalar",
                       waits=(stores_all[-1],), slot="dbg")
                    D_(s6, dbgout("rwf", (128, 32)), rwf[:], "scalar",
                       waits=(stores_all[-1],), slot="dbg")
                    D_(s6, dbgout("csb", (128, 2 * NTT)), csb[:], "scalar",
                       waits=(stores_all[-1],), slot="dbg")
                    D_(s6, dbgout("st8", (128, 32)), st8[:], "scalar",
                       waits=(stores_all[-1],), slot="dbg")
                    D_(s6, dbgout("gatedT", (512, NBq * 512), BF16),
                       s_gatedT[:, 0:NBq * 512], "sync",
                       waits=tuple(evcoll) + (stores_all[-1],), slot="dbg3")
                with nc.Block("p2") as blk:
                    s6.emit(blk)

        ESC.close()

        s_kb = scr("s_kb", (1, D), F32)
        if "P3A" in active:
            P7 = ExitStack()
            with P7:
                sb7 = lambda n, sh, dt: P7.enter_context(
                    nc.sbuf_tensor(n, list(sh), dt))
                KHW = NT * 64
                NKT = KHW // 128
                NGR = NTO // 2
                kH = sb7("kH", (128, 8 * KHW), BF16)
                vH = sb7("vH", (128, NKT * 1024), BF16)
                qTg = [sb7(f"qTg{i}", (128, 8 * 256), BF16) for i in range(2)]
                aT = [sb7(f"aT{i}", (128, 256), BF16) for i in range(2)]
                sbneg = sb7("sbneg", (1, 256), BF16)
                sbf = sb7("sbf", (1, 256), F32)
                kbf = sb7("kbf", (128, 8), F32)
                ggb = [sb7(f"ggb{i}", (128, 1024), BF16) for i in range(2)]
                wdn = sb7("wdn", (128, 8 * 1024), BF16)
                pstg = [sb7(f"pstg{i}", (128, 1028), F32) for i in range(2)]
                pasb = [sb7(f"pasb{i}", (128, 1028), F32) for i in range(2)]
                tsum = sb7("tsum", (128, 2048), F32)
                dcol = sb7("dcol", (128, 4), F32)
                ystg = [sb7(f"ystg{i}", (128, 1024), F32) for i in range(2)]
                xob = [sb7(f"xob{i}", (128, 1024), F32) for i in range(2)]

                pS7 = [P7.enter_context(nc.psum_tensor(f"pS7{i}", [128, 512],
                                                       F32))
                       for i in range(2)]
                pO7 = [P7.enter_context(nc.psum_tensor(f"pO7{i}", [128, 512],
                                                       F32))
                       for i in range(4)]
                pD7 = P7.enter_context(nc.psum_tensor("pD7", [128, 512], F32))
                pB7 = P7.enter_context(nc.psum_tensor("pB7", [1, 256], F32))

                evkb_hold = [None]
                for half in range(2):
                    final = (half == 1) and ("P3B" in active)
                    s7 = Sched(sems, makesem)
                    if half == 0:
                        ek1 = D_(s7, s_kb, ksum_sb[0:1, :], "sync",
                                 slot="kb")
                        ek2 = s7.step(
                            "sync",
                            lambda e: e.dma_start(
                                out=kbf[:],
                                in_=s_kb.rearrange("o (c p) -> (o p) c",
                                                   p=128),
                                allow_slow_non_contiguous=True),
                            waits=(ek1,), dma="kb")
                        ek3 = s7.step(
                            "vector",
                            lambda e: e.tensor_scalar(
                                kbf[:], kbf[:], 1.0 / S, None, ALU.mult),
                            waits=(ek2,))
                        evkb = s7.step(
                            "vector",
                            lambda e: e.tensor_copy(kbar_bf[:], kbf[:]),
                            waits=(ek3,))
                    else:
                        evkb = None
                    if final and NT != 64:
                        for i4 in range(4):
                            s7.step(
                                "sync",
                                lambda e, i4=i4: e.dma_start(
                                    out=bass.AP(g_gath[i4].tensor,
                                                regs["fb"],
                                                [[S, 128], [1, NT * 128]]),
                                    in_=s_gatedT[i4 * 128:(i4 + 1) * 128,
                                                 0:NT * 128]),
                                dma="ggf")
                    evKH = D_(s7, kH[:].rearrange("p (c t) -> p c t", t=KHW),
                              s_ffnkT.rearrange("(c p) t -> p c t", p=128)
                              [:, :, half * KHW:(half + 1) * KHW],
                              "sync", slot="ffk")
                    evVH = D_(s7, vH[:].rearrange("p (t d) -> p t d", d=1024),
                              s_ffnv[half * KHW:(half + 1) * KHW, :]
                              .rearrange("(t p) d -> p t d", p=128),
                              "sync", slot="ffv")
                    evwd = None
                    evstg = None
                    if final:
                        evwd = D_(s7,
                                  wdn[:].rearrange("p (c f) -> p c f",
                                                   f=1024),
                                  i_wdown.rearrange("(c p) f -> p c f",
                                                    p=128),
                                  "gpsimd", slot="wd")
                        ev_ = []
                        for i4 in range(4):
                            ev_.append(s7.step(
                                "sync",
                                lambda e, i4=i4: e.dma_start(
                                    out=s_gg[i4][:, 0:NTO * 128],
                                    in_=bass.AP(g_gath[i4].tensor,
                                                regs["goff"],
                                                [[S, 256], [1, NTO * 128]])),
                                dma="stg"))
                        evstg = tuple(ev_)
                    ev_pS_free = {0: None, 1: None}
                    ev_aT_free = {0: None, 1: None}
                    ev_qT_free = {0: None, 1: None}
                    ev_pO_free = None
                    ev_pD_free = None
                    ev_pB_free = None
                    ev_y_prev = {}
                    for g2 in range(NGR):
                        gp = g2 % 2
                        evq7 = D_(s7, qTg[gp][:].rearrange(
                                      "p (c t) -> p c t", t=256),
                                  s_ffnqT.rearrange("(c p) t -> p c t",
                                                    p=128)
                                  [:, :, g2 * 256:(g2 + 1) * 256],
                                  "sync", waits=(ev_qT_free[gp],),
                                  slot=f"fq{gp}")

                        def mm_sb(e, gp=gp):
                            last = None
                            for c in range(8):
                                last = e.matmul(
                                    pB7[:], kbar_bf[:, c:c + 1],
                                    qTg[gp][:, c * 256:(c + 1) * 256],
                                    start=(c == 0), stop=(c == 7))
                            return last
                        evsb = s7.step("tensor", mm_sb,
                                       waits=(evq7, evkb, ev_pB_free))
                        evsb2 = s7.step(
                            "vector",
                            lambda e: e.tensor_scalar(
                                sbf[:], pB7[:], -1.0, None, ALU.mult),
                            waits=(evsb,))
                        ev_pB_free = evsb2
                        evsb3 = s7.step(
                            "vector",
                            lambda e: e.tensor_copy(sbneg[:], sbf[:]),
                            waits=(evsb2,))
                        ev_s7 = None
                        ev_v7 = None
                        for tau in range(NKT):
                            par = tau % 2

                            def mm_s7(e, gp=gp, tau=tau, par=par):
                                for c in range(8):
                                    e.matmul(
                                        pS7[par][:, 0:256],
                                        kH[:, c * KHW + tau * 128:
                                           c * KHW + (tau + 1) * 128],
                                        qTg[gp][:, c * 256:(c + 1) * 256],
                                        start=(c == 0), stop=False)
                                return e.matmul(
                                    pS7[par][:, 0:256], ones_bf[0:1, :],
                                    sbneg[:], start=False, stop=True)
                            ev_s7 = s7.step(
                                "tensor", mm_s7,
                                waits=(evKH, evsb3, ev_pS_free[par]))
                            ev_e7 = s7.step(
                                "scalar",
                                lambda e, par=par: e.activation(
                                    aT[par][:], pS7[par][:, 0:256], AF.Exp,
                                    scale=1.0 / 32.0),
                                waits=(ev_s7, ev_aT_free[par]))
                            ev_pS_free[par] = ev_e7

                            def mm_v7(e, tau=tau, par=par):
                                last = None
                                for j in range(2):
                                    for dh in range(2):
                                        last = e.matmul(
                                            pO7[2 * j + dh][:],
                                            aT[par][:,
                                                    j * 128:(j + 1) * 128],
                                            vH[:, tau * 1024 + dh * 512:
                                               tau * 1024 + (dh + 1) * 512],
                                            start=(tau == 0),
                                            stop=(tau == NKT - 1))
                                    last = e.matmul(
                                        pD7[:, j * 132:j * 132 + 1],
                                        aT[par][:, j * 128:(j + 1) * 128],
                                        ones_bf[:, 0:1],
                                        start=(tau == 0 and j == 0),
                                        stop=(tau == NKT - 1 and j == 1))
                                return last
                            w7 = (ev_e7, evVH)
                            if tau == 0:
                                w7 = w7 + (ev_pO_free, ev_pD_free)
                            ev_v7 = s7.step("tensor", mm_v7, waits=w7)
                            ev_aT_free[par] = ev_v7
                        ev_qT_free[gp] = ev_s7
                        if not final:
                            evo = []
                            for j in range(2):
                                wpr = ev_y_prev.get((g2 - 1, j))
                                e1_ = s7.step(
                                    "vector",
                                    lambda e, j=j: e.tensor_copy(
                                        pstg[j][:, 0:512], pO7[2 * j][:]),
                                    waits=(ev_v7, wpr))
                                e2_ = s7.step(
                                    "vector",
                                    lambda e, j=j: e.tensor_copy(
                                        pstg[j][:, 512:1024],
                                        pO7[2 * j + 1][:]),
                                    waits=(ev_v7, wpr))
                                e3_ = s7.step(
                                    "vector",
                                    lambda e, j=j: e.tensor_copy(
                                        pstg[j][:, 1024:1025],
                                        pD7[:, j * 132:j * 132 + 1]),
                                    waits=(ev_v7, wpr))
                                ev = D_(s7, s_pA[2 * g2 + j][:, 0:1025],
                                        pstg[j][:, 0:1025], "scalar",
                                        waits=(e1_, e2_, e3_),
                                        slot=f"pa{j}")
                                evo.append(ev)
                                ev_y_prev[(g2, j)] = ev
                            ev_pO_free = evo[-1]
                            ev_pD_free = evo[-1]
                        else:
                            ey2 = None
                            ec3 = None
                            for j in range(2):
                                tile = 2 * g2 + j
                                jp = j
                                wpr = ev_y_prev.get((g2 - 1, j))
                                evpa = D_(s7, pasb[jp][:, 0:1025],
                                          s_pA[tile][:, 0:1025],
                                          "sync", waits=(wpr,),
                                          slot=f"lpa{jp}")
                                evxo = D_(s7, xob[jp][:],
                                          i_xo[tile * 128:
                                               (tile + 1) * 128, :],
                                          "sync", waits=(wpr,),
                                          slot=f"xo{jp}")
                                evgg = []
                                for i4 in range(4):
                                    evgg.append(s7.step(
                                        "sync",
                                        lambda e, i4=i4, jp=jp, tile=tile:
                                        e.dma_start(
                                            out=ggb[jp][:].rearrange(
                                                "p (i t) -> p i t", t=128)
                                            [:, 2 * i4:2 * i4 + 2, :],
                                            in_=s_gg[i4]
                                            [:, tile * 128:
                                             (tile + 1) * 128]
                                            .rearrange("(s p) t -> p s t",
                                                       p=128)),
                                        waits=evstg + (wpr,),
                                        dma=f"gg{jp}_{i4}"))
                                evgg = tuple(evgg)
                                ts_ = tsum[:, j * 1024:(j + 1) * 1024]
                                ec1 = s7.step(
                                    "vector",
                                    lambda e, j=j, jp=jp, ts_=ts_:
                                    e.tensor_tensor(
                                        ts_[:, 0:512], pasb[jp][:, 0:512],
                                        pO7[2 * j][:], ALU.add),
                                    waits=(ev_v7, evpa, wpr))
                                ec2 = s7.step(
                                    "vector",
                                    lambda e, j=j, jp=jp, ts_=ts_:
                                    e.tensor_tensor(
                                        ts_[:, 512:1024],
                                        pasb[jp][:, 512:1024],
                                        pO7[2 * j + 1][:], ALU.add),
                                    waits=(ev_v7, evpa))
                                ec3 = s7.step(
                                    "vector",
                                    lambda e, j=j, jp=jp: e.tensor_tensor(
                                        dcol[:, 2 * j:2 * j + 1],
                                        pasb[jp][:, 1024:1025],
                                        pD7[:, j * 132:j * 132 + 1],
                                        ALU.add),
                                    waits=(ev_v7, evpa, wpr))
                                ec4 = s7.step(
                                    "vector",
                                    lambda e, j=j: e.reciprocal(
                                        dcol[:, 2 * j + 1:2 * j + 2],
                                        dcol[:, 2 * j:2 * j + 1]),
                                    waits=(ec3,))

                                def mm_y(e, jp=jp):
                                    last = None
                                    for c8 in range(8):
                                        for dh in range(2):
                                            last = e.matmul(
                                                pS7[dh][:],
                                                ggb[jp][:, c8 * 128:
                                                        (c8 + 1) * 128],
                                                wdn[:, c8 * 1024 +
                                                    dh * 512:
                                                    c8 * 1024 +
                                                    (dh + 1) * 512],
                                                start=(c8 == 0),
                                                stop=(c8 == 7))
                                    return last
                                evy1 = s7.step(
                                    "tensor", mm_y,
                                    waits=evgg + (evwd, ev_pS_free[0],
                                                  ev_pS_free[1], ec1, ec2))
                                eu1 = s7.step(
                                    "vector",
                                    lambda e, jp=jp, j=j, ts_=ts_:
                                    e.scalar_tensor_tensor(
                                        ystg[jp][:], ts_,
                                        dcol[:, 2 * j + 1:2 * j + 2],
                                        xob[jp][:], ALU.mult, ALU.add),
                                    waits=(ec1, ec2, ec4, evxo, wpr))
                                ey1 = s7.step(
                                    "vector",
                                    lambda e, jp=jp: e.tensor_tensor(
                                        ystg[jp][:, 0:512],
                                        ystg[jp][:, 0:512], pS7[0][:],
                                        ALU.add),
                                    waits=(eu1, evy1))
                                ey2 = s7.step(
                                    "vector",
                                    lambda e, jp=jp: e.tensor_tensor(
                                        ystg[jp][:, 512:1024],
                                        ystg[jp][:, 512:1024], pS7[1][:],
                                        ALU.add),
                                    waits=(eu1, evy1))
                                ev_pS_free[0] = ey1
                                ev_pS_free[1] = ey2
                                evst = D_(s7,
                                          o_y[tile * 128:
                                              (tile + 1) * 128, :],
                                          ystg[jp][:], "scalar",
                                          waits=(ey1, ey2), slot=f"y{jp}")
                                ev_y_prev[(g2, j)] = evst
                            ev_pO_free = ey2
                            ev_pD_free = ec3
                    with nc.Block(f"p3{half}") as blk:
                        s7.emit(blk)

        if debug:
            s9 = Sched(sems, makesem)
            SW = NT * 128
            SWO = NTO * 128
            pairs = []
            if "P1A" in active:
                pairs += [("ffnv", s_ffnv[0:SW, :], (SW, D)),
                          ("ffnkT", s_ffnkT[:, 0:SW], (D, SW)),
                          ("mixqT", s_mixqT[:, 0:SW], (256, SW)),
                          ("mixkT", s_mixkT[:, 0:SW], (256, SW)),
                          ("sigoT", s_sigoT[:, 0:SW], (512, SW)),
                          ("mixv", s_mixv[:, 0:SW, :], (4, SW, 128))]
            if "P1B" in active:
                pairs += [("ffnqT", s_ffnqT[:, 0:SWO], (D, SWO))]
            for nm, src, shp in pairs:
                D_(s9, dbgout(nm, shp, BF16), src, "sync", slot="dbg2")
            with nc.Block("dbgcopy") as blk:
                s9.emit(blk)

        if "P3B" not in active:
            s9b = Sched(sems, makesem)
            zb = sbuf("zb", (128, 1024), F32)
            s9b.step("vector", lambda e: e.memset(zb[:], 0.0))
            for t in range(NTO):
                D_(s9b, o_y[t * 128:(t + 1) * 128, :], zb[:], "sync", slot="sty")
            with nc.Block("yfill") as blk:
                s9b.emit(blk)

    return nc, dbg



def _consts(p, b):
    ident = np.eye(128, dtype=np.float32)
    onesb = np.ones((128, 128), np.float32)
    m = np.zeros((128, 4 * 512), np.float32)
    t = np.arange(128)[:, None]
    q = np.arange(512)[None, :]
    for j in range(4):
        m[:, j * 512:(j + 1) * 512] = (128 * j + t) <= q
    iota4 = np.broadcast_to(np.arange(4, dtype=np.float32), (128, 4))
    iota16 = np.broadcast_to(np.arange(16, dtype=np.float32), (128, 16))
    offg = np.array([[256 * b * S + p * SH]], np.int32)
    offr = np.array([[(2 * b + p) * 128 * S]], np.int32)
    return dict(
        offr=offr,
        idbf=ident.astype(ml_dtypes.bfloat16),
        idf32=ident,
        onesbf=onesb.astype(ml_dtypes.bfloat16),
        masks=m.astype(ml_dtypes.bfloat16),
        iota4=np.ascontiguousarray(iota4),
        iota16=np.ascontiguousarray(iota16),
        offg=offg,
    )


def make_inmaps(inputs):
    x = np.asarray(inputs["L0_attn_x"])
    wi, wf = np.asarray(inputs["L0_attn_Wi"]), np.asarray(inputs["L0_attn_Wf"])
    bi = np.asarray(inputs["L0_attn_bi"])
    bfv = np.asarray(inputs["L0_attn_bf"])
    nodew = np.concatenate(
        [np.concatenate([np.asarray(inputs[f"L0_ffn_{f}_node_w"]).T,
                         np.zeros((D, 1), np.float32)], axis=1)
         for f in "qkv"], axis=1)
    nodeb = np.concatenate(
        [np.concatenate([np.asarray(inputs[f"L0_ffn_{f}_node_b"]),
                         np.zeros(1, np.float32)]) for f in "qkv"])[None, :]
    w1 = {f: np.asarray(inputs[f"L0_ffn_{f}_w1"]).transpose(1, 0, 2)
          .reshape(D, D) for f in "qkv"}
    w2 = {f: np.asarray(inputs[f"L0_ffn_{f}_w2"]).reshape(D, D) for f in "qkv"}
    b1 = {f: np.asarray(inputs[f"L0_ffn_{f}_b1"]).reshape(1, D) for f in "qkv"}
    b2 = {f: np.asarray(inputs[f"L0_ffn_{f}_b2"]) for f in "qkv"}
    perm = []
    for i in range(4):
        perm += [i, 4 + i]
    rows = np.concatenate([np.arange(h * 128, (h + 1) * 128) for h in perm])
    wdown_p = np.asarray(inputs["L0_attn_Wdown"])[rows]

    maps = []
    for c in range(8):
        b, p = c // 2, c % 2
        hs = slice(p * 256, (p + 1) * 256)
        hv = slice(p * 512, (p + 1) * 512)
        m = dict(
            xf=np.ascontiguousarray(x[b]),
            xo=np.ascontiguousarray(x[b, p * SH:(p + 1) * SH]),
            wq=np.ascontiguousarray(np.asarray(inputs["L0_attn_Wq"])[:, hs]),
            wk=np.ascontiguousarray(np.asarray(inputs["L0_attn_Wk"])[:, hs]),
            wv=np.ascontiguousarray(np.asarray(inputs["L0_attn_Wv"])[:, hv]),
            wo=np.ascontiguousarray(np.asarray(inputs["L0_attn_Wo"])[:, hv]),
            wif=np.ascontiguousarray(
                np.concatenate([wi[:, p * 4:(p + 1) * 4],
                                wf[:, p * 4:(p + 1) * 4]], axis=1)),
            bif=np.ascontiguousarray(
                np.concatenate([bi[p * 4:(p + 1) * 4],
                                bfv[p * 4:(p + 1) * 4]])[None, :]),
            nodew=np.ascontiguousarray(nodew),
            nodeb=np.ascontiguousarray(nodeb),
            wdown=np.ascontiguousarray(wdown_p),
            onorm=np.ascontiguousarray(
                np.asarray(inputs["L0_attn_outnorm_w"])[p * 4:(p + 1) * 4].T),
        )
        for f in "qkv":
            m[f"w1{f}"] = np.ascontiguousarray(w1[f])
            m[f"w2{f}"] = np.ascontiguousarray(w2[f])
            m[f"b1{f}"] = np.ascontiguousarray(b1[f])
            m[f"b2{f}"] = np.ascontiguousarray(b2[f])
        m.update(_consts(p, b))
        maps.append(m)
    return maps


def kernel(**inputs):
    nc, _ = build()
    maps = make_inmaps(inputs)
    res = run_bass_kernel_spmd(nc, maps, core_ids=list(range(8)))
    y = np.empty((B, S, D), np.float32)
    for c in range(8):
        b, p = c // 2, c % 2
        y[b, p * SH:(p + 1) * SH] = res.results[c]["y"]
    return y
